# version 03
# baseline (speedup 1.0000x reference)
import numpy as np
import concourse.bass as bass
import concourse.mybir as mybir
from concourse.bass_utils import run_bass_kernel_spmd
from concourse.alu_op_type import AluOpType

F32 = mybir.dt.float32
BF16 = mybir.dt.bfloat16
AF = mybir.ActivationFunctionType
AX = mybir.AxisListType
OP = AluOpType

T = 4096
D = 1024
DI = 2048
NS = 16
WIN = 16
NW = T // WIN
DT = D // 128
EI = DI // 128
H = 16
HD = 64
NE = 16
TOPK = 6
EXP_NEG10 = float(np.exp(np.float32(-10.0)))
LN10 = float(np.log(10.0))
NCORES = 8

ALL_PHASES = ("M1a", "M1b", "SCAN", "FFN1", "ATTB", "FFN2", "OUT")


class Prog:
    ENGS = ("v", "s", "t", "g", "q")

    def __init__(self, nc):
        self.nc = nc
        self.items = {e: [] for e in self.ENGS}
        self.ngrp = {}
        self.pending = {e: [] for e in self.ENGS}

    def op(self, e, fn, tag=""):
        it = {"fn": fn, "mark": False, "dma": False,
              "waits": self.pending[e], "tag": tag}
        self.pending[e] = []
        self.items[e].append(it)
        return (e, len(self.items[e]) - 1)

    def dma(self, e, out, in_, g=None, **kw):
        def fn(eng):
            return eng.dma_start(out=out, in_=in_, **kw)
        g = g if g is not None else "q_" + e
        self.ngrp[g] = self.ngrp.get(g, 0) + 1
        it = {"fn": fn, "mark": False, "dma": True, "grp": g,
              "waits": self.pending[e], "tag": "dma"}
        self.pending[e] = []
        self.items[e].append(it)
        return ("DMA", g, self.ngrp[g])

    def wait(self, e, tok):
        if tok is None or tok[0] == e:
            return
        self.pending[e].append(tok)

    def wait_many(self, e, toks):
        for t in toks:
            self.wait(e, t)

    def fence(self, e):
        if self.pending[e]:
            self.op(e, (lambda eng: eng.nop()), tag="fence")

    def last(self, e):
        return (e, len(self.items[e]) - 1)

    def last_real(self, e):
        for i in range(len(self.items[e]) - 1, -1, -1):
            it = self.items[e][i]
            if it["fn"] is not None and it["tag"] not in ("fence", "barnop"):
                return (e, i)
        return None

    def barrier(self, engs=None):
        engs = [e for e in (engs or self.ENGS) if self.items[e]]
        toks = {}
        for e in engs:
            self.fence(e)
            toks[e] = self.last_real(e)
        for e in engs:
            for e2 in engs:
                if e2 != e:
                    self.wait(e, toks[e2])
            self.fence(e)

    def build_block(self):
        nc = self.nc
        for g, n in self.ngrp.items():
            self.pending["q"].append(("DMA", g, n))
        self.fence("q")
        csem = {e: nc.alloc_semaphore(f"c_{e}") for e in self.ENGS}
        gsem = {g: nc.alloc_semaphore(f"g_{g}") for g in self.ngrp}
        for e in self.ENGS:
            for it in self.items[e]:
                for tok in it["waits"]:
                    if tok[0] in self.ENGS:
                        self.items[tok[0]][tok[1]]["mark"] = True
        markcnt = {}
        for e in self.ENGS:
            c = 0
            counts = []
            for it in self.items[e]:
                if it["mark"] and not it["dma"]:
                    c += 1
                counts.append(c)
            markcnt[e] = counts

        class _DrainGuard:

            PASSTHRU = {"wait_ge", "drain", "nop", "then_inc", "dma_start",
                        "sem_inc", "memzero"}

            def __init__(self, eng):
                self._eng = eng
                self._pw = set()
                self._pr = set()

            @staticmethod
            def _aps(vals):
                names = set()
                for v in vals:
                    t = getattr(v, "tensor", None)
                    if t is not None:
                        names.add(t.name)
                return names

            def __getattr__(self, name):
                real = getattr(self._eng, name)
                if name in self.PASSTHRU or not callable(real):
                    return real

                def call(*a, **kw):
                    if "out" in kw:
                        writes = self._aps([kw["out"], kw.get("accum_out")])
                        reads = self._aps(list(a) + [v for k2, v in kw.items()
                                                     if k2 not in ("out", "accum_out")])
                    elif a:
                        writes = self._aps([a[0]])
                        reads = self._aps(list(a[1:]) + list(kw.values()))
                    else:
                        writes, reads = set(), self._aps(kw.values())
                    if ((writes | reads) & self._pw) or (writes & self._pr):
                        self._eng.drain()
                    inst = real(*a, **kw)
                    self._pw = writes
                    self._pr = reads
                    return inst
                return call

        def run_stream(eng, e):
            if e == "v":
                eng = _DrainGuard(eng)
            for it in self.items[e]:
                resolved = {}
                for tok in it["waits"]:
                    if tok[0] == "DMA":
                        key = ("g", tok[1])
                        val = 16 * tok[2]
                    elif tok[0] in self.ENGS:
                        src = tok[0]
                        sit = self.items[src][tok[1]]
                        if sit["dma"]:
                            g = sit["grp"]
                            c = 0
                            for j in range(tok[1] + 1):
                                jt = self.items[src][j]
                                if jt["dma"] and jt["grp"] == g:
                                    c += 1
                            key = ("g", g)
                            val = 16 * c
                        else:
                            key = ("c", src)
                            val = markcnt[src][tok[1]]
                    resolved[key] = max(resolved.get(key, 0), val)
                for (kind, sid), val in resolved.items():
                    eng.wait_ge(gsem[sid] if kind == "g" else csem[sid], val)
                if it["fn"] is None:
                    continue
                inst = it["fn"](eng)
                if it["dma"]:
                    inst.then_inc(gsem[it["grp"]], 16)
                elif it["mark"]:
                    inst.then_inc(csem[e], 1)

        with nc.Block() as block:
            @block.vector
            def _(eng):
                run_stream(eng, "v")

            @block.scalar
            def _(eng):
                run_stream(eng, "s")

            @block.tensor
            def _(eng):
                run_stream(eng, "t")

            @block.gpsimd
            def _(eng):
                run_stream(eng, "g")

            @block.sync
            def _(eng):
                run_stream(eng, "q")


class Arena:
    def __init__(self, nc, base, top):
        self.nc = nc
        self.off = base
        self.top = top
        self.n = 0
        self.marks = []

    def alloc(self, shape, dtype, name=None):
        free = int(np.prod(shape[1:])) if len(shape) > 1 else 1
        nbytes = free * mybir.dt.size(dtype)
        off = (self.off + 31) & ~31
        assert off + nbytes <= self.top, \
            f"SBUF overflow: {name} needs {nbytes}B at {off}, top={self.top}"
        self.off = off + nbytes
        self.n += 1
        h = self.nc.alloc_sbuf_tensor_at(
            name or f"ar{self.n}", list(shape), dtype, offset=off)
        return h.ap()

    def push(self):
        self.marks.append(self.off)

    def pop(self):
        self.off = self.marks.pop()


def _prep_shared(inputs):
    d = {}
    d["in_proj_T"] = np.asarray(inputs["L0_attn_a_in_proj_w"]).T
    d["in_proj_b"] = np.asarray(inputs["L0_attn_a_in_proj_b"]).reshape(2 * EI, 128).T
    d["conv_w"] = np.asarray(inputs["L0_attn_a_conv_w"]).reshape(DI, 4).reshape(EI, 128, 4).transpose(1, 0, 2)
    d["conv_b"] = np.asarray(inputs["L0_attn_a_conv_b"]).reshape(EI, 128).T
    d["x_proj_T"] = np.asarray(inputs["L0_attn_a_x_proj_w"]).T
    d["x_proj_b"] = np.asarray(inputs["L0_attn_a_x_proj_b"]).reshape(2 * NS, 1)
    d["dt_proj_T"] = np.asarray(inputs["L0_attn_a_dt_proj_w"]).T
    d["dt_proj_b"] = np.asarray(inputs["L0_attn_a_dt_proj_b"]).reshape(EI, 128).T
    d["A_log"] = np.asarray(inputs["L0_attn_a_A_log"]).reshape(EI, 128, NS).transpose(1, 0, 2)
    d["D_param"] = np.asarray(inputs["L0_attn_a_D_param"]).reshape(EI, 128).T
    d["out_proj_T"] = np.asarray(inputs["L0_attn_a_out_proj_w"]).T
    d["out_proj_b"] = np.asarray(inputs["L0_attn_a_out_proj_b"]).reshape(DT, 128).T
    d["norm_alpha"] = np.asarray(inputs["L0_attn_a_norm_alpha"]).reshape(DT, 128).T
    d["norm_bias"] = np.asarray(inputs["L0_attn_a_norm_bias"]).reshape(DT, 128).T
    d["in_norm_alpha"] = np.asarray(inputs["L0_attn_a_in_norm_alpha"]).reshape(DT, 128).T
    d["in_norm_bias"] = np.asarray(inputs["L0_attn_a_in_norm_bias"]).reshape(DT, 128).T
    for nm in ("Wq", "Wk", "Wv", "Wo"):
        d[nm + "_T"] = np.asarray(inputs[f"L0_attn_b_{nm}"]).T
    for nm in ("bq", "bk", "bv", "bo"):
        d[nm] = np.asarray(inputs[f"L0_attn_b_{nm}"]).reshape(DT, 128).T
        d[nm + "_row"] = np.asarray(inputs[f"L0_attn_b_{nm}"]).reshape(1, D)
    d["router_T"] = np.asarray(inputs["L0_ffn_router_w"]).T
    d["router_b"] = np.asarray(inputs["L0_ffn_router_b"]).reshape(1, NE)
    d["expert_w"] = np.asarray(inputs["L0_ffn_expert_w"])
    return {k: np.ascontiguousarray(np.asarray(v), dtype=np.float32)
            for k, v in d.items()}


def build_program(phases=ALL_PHASES, dbg=False, noinj=False):
    nc = bass.Bass("TRN2", target_bir_lowering=False, detect_race_conditions=False)
    p = Prog(nc)

    di = {}

    def din(name, shape):
        di[name] = nc.dram_tensor(name, list(shape), F32,
                                  kind="ExternalInput").ap()

    din("x_fm", [D, T])
    din("in_proj_T", [D, 2 * DI]); din("in_proj_b", [128, 2 * EI])
    din("conv_w", [128, EI, 4]); din("conv_b", [128, EI])
    din("x_proj_T", [DI, 2 * NS]); din("x_proj_b", [2 * NS, 1])
    din("dt_proj_T", [DI, DI]); din("dt_proj_b", [128, EI])
    din("A_log", [128, EI, NS]); din("D_param", [128, EI])
    din("out_proj_T", [DI, D]); din("out_proj_b", [128, DT])
    din("norm_alpha", [128, DT]); din("norm_bias", [128, DT])
    din("in_norm_alpha", [128, DT]); din("in_norm_bias", [128, DT])
    for nm in ("Wq", "Wk", "Wv", "Wo"):
        din(nm + "_T", [D, D])
    for nm in ("bq", "bk", "bv", "bo"):
        din(nm, [128, DT])
        din(nm + "_row", [1, D])
    din("router_T", [D, NE]); din("router_b", [1, NE])
    din("expert_w", [NE, D, D])

    out_dram = nc.dram_tensor("out_tm", [T, D], F32, kind="ExternalOutput").ap()

    ds = {}

    def dsc(name, shape, dtype=F32):
        kind = "ExternalOutput" if dbg else "Internal"
        ds[name] = nc.dram_tensor(name, list(shape), dtype, kind=kind).ap()

    dsc("r_res", [D, T])
    dsc("r_xn1", [D, T], BF16)
    dsc("r_dt", [DI, T])
    dsc("r_dtx", [DI, T], BF16)
    dsc("r_xc", [DI, T])
    dsc("r_xcb", [DI, T], BF16)
    dsc("r_sz", [DI, T])
    dsc("r_bc", [2 * NS, T])
    dsc("r_A", [128, EI, NS])
    dsc("r_P", [EI * NS, 128, NW], BF16)
    dsc("r_q", [EI * NS, 128, NW], BF16)
    dsc("r_H", [EI * NS, 128, NW + 1], BF16)
    if dbg:
        dsc("r_drg", [T // 128, 128, NE])
        dsc("r_dpr", [T // 128, 128, NE])
        dsc("r_dt8", [T // 128, 128, 8])
        dsc("r_dyg", [DI, T], BF16)
        dsc("r_dt1", [D, T])
        dsc("r_dh", [DI, T], BF16)
    dsc("r_qb", [D, T], BF16)
    dsc("r_kb", [T, D], BF16)
    dsc("r_vb", [T, D], BF16)
    dsc("r_xnf", [D, T])

    c_ident = nc.inline_tensor(np.eye(128, dtype=np.float32), name="c_ident").ap()
    c_ones_col = nc.inline_tensor(np.ones((128, 1), np.float32), name="c_onescol").ap()
    _m1 = np.zeros((128, DT, H), np.float32)
    _m2 = np.zeros((H, DT, 128), np.float32)
    for h_ in range(H):
        k_, half = h_ // 2, h_ % 2
        _m1[half * 64:(half + 1) * 64, k_, h_] = 1.0
        _m2[h_, k_, half * 64:(half + 1) * 64] = 1.0
    c_m1 = nc.inline_tensor(_m1, name="c_m1").ap()
    c_m2 = nc.inline_tensor(_m2, name="c_m2").ap()

    AR_BASE = 24 * 1024
    AR_TOP = 224 * 1024
    ar = Arena(nc, AR_BASE, AR_TOP)

    pb = [nc.alloc_psum_tensor(f"pb{i}", [128, 512], F32).ap() for i in range(8)]

    _ccols = {}
    sb_cc = ar.alloc([128, 16], F32, "constcols")
    def ccol(val):
        val = float(val)
        if val not in _ccols:
            idx = len(_ccols)
            assert idx < 16
            ap = sb_cc[:, idx:idx + 1]
            p.op("v", lambda e, ap=ap, val=val: e.memset(ap, val))
            _ccols[val] = ap
        return _ccols[val]

    sb_ident_bf = ar.alloc([128, 128], BF16, "ident_bf")
    sb_ones_col_bf = ar.alloc([128, 1], BF16, "ones_col_bf")
    sb_ones_row_bf = ar.alloc([1, 128], BF16, "ones_row_bf")
    c_ones_row = nc.inline_tensor(np.ones((1, 128), np.float32), name="c_onesrow").ap()
    for _cv in (1e-5, -0.5, -1.0, 1.0, LN10, 0.5):
        ccol(_cv)
    t_i1 = p.dma("g", sb_ident_bf, c_ident, g="cst")
    t_i2 = p.dma("g", sb_ones_col_bf, c_ones_col, g="cst")
    t_i3 = p.dma("g", sb_ones_row_bf, c_ones_row, g="cst")
    base_toks = [t_i1, t_i2, t_i3]
    p.wait_many("t", base_toks)

    TCM = 512
    NCM = T // TCM

    def emit_ln0(xch, xnb, tc, xb_tmp, sqb, rows, rows_bf, dep_toks,
                 pm=pb[6], pv=pb[7], pmb=pb[4], prb=pb[5], out_f32=None):
        p.wait_many("v", dep_toks)
        tcast = p.op("v", lambda e: e.tensor_copy(
            xb_tmp.rearrange("p k t -> p (k t)"), xch.rearrange("p k t -> p (k t)")))
        p.wait_many("s", dep_toks)
        tsq = p.op("s", lambda e: e.activation(
            out=sqb.rearrange("p k t -> p (k t)"),
            in_=xch.rearrange("p k t -> p (k t)"), func=AF.Square))
        p.wait("t", tcast)
        p.wait("t", tsq)
        for k in range(DT):
            p.op("t", lambda e, k=k: e.matmul(
                pm[0:1, 0:tc], sb_ones_col_bf, xb_tmp[:, k],
                start=(k == 0), stop=(k == DT - 1)))
        for k in range(DT):
            tmm = p.op("t", lambda e, k=k: e.matmul(
                pv[0:1, 0:tc], sb_ones_col_bf, sqb[:, k],
                start=(k == 0), stop=(k == DT - 1)))
        m_row = rows[:, 0:tc]
        var_row = rows[:, tc:2 * tc]
        m2_row = rows[:, 2 * tc:3 * tc]
        p.wait("v", tmm)
        p.op("v", lambda e: e.tensor_scalar(
            out=m_row, in0=pm[0:1, 0:tc], scalar1=1.0 / D, scalar2=None,
            op0=OP.mult))
        p.op("v", lambda e: e.tensor_tensor(
            out=m2_row, in0=m_row, in1=m_row, op=OP.mult))
        tvar = p.op("v", lambda e: e.scalar_tensor_tensor(
            out=var_row, in0=pv[0:1, 0:tc], scalar=1.0 / D, in1=m2_row,
            op0=OP.mult, op1=OP.subtract))
        p.wait("s", tvar)
        c_eps = ccol(1e-5)[0:1, :]
        c_mhalf = ccol(-0.5)[0:1, :]
        tln = p.op("s", lambda e, b=c_eps: e.activation(
            out=rows[:, 3 * tc:4 * tc], in_=var_row, func=AF.Ln, bias=b))
        trs = p.op("s", lambda e, s_=c_mhalf: e.activation(
            out=rows_bf[:, tc:2 * tc], in_=rows[:, 3 * tc:4 * tc],
            func=AF.Exp, scale=s_))
        tmb = p.op("s", lambda e: e.activation(
            out=rows_bf[:, 0:tc], in_=m_row, func=AF.Copy))
        p.wait("t", trs)
        p.wait("t", tmb)
        t1 = p.op("t", lambda e: e.matmul(
            pmb[:, 0:tc], sb_ones_row_bf, rows_bf[:, 0:tc], start=True, stop=True))
        t2 = p.op("t", lambda e: e.matmul(
            prb[:, 0:tc], sb_ones_row_bf, rows_bf[:, tc:2 * tc], start=True, stop=True))
        p.wait("v", t1)
        p.wait("v", t2)
        tgt = out_f32 if out_f32 is not None else xch
        p.op("v", lambda e: e.tensor_tensor(
            out=tgt, in0=xch,
            in1=pmb[:, 0:tc].unsqueeze(1).to_broadcast([128, DT, tc]),
            op=OP.subtract))
        tok = p.op("v", lambda e: e.tensor_tensor(
            out=xnb, in0=tgt,
            in1=prb[:, 0:tc].unsqueeze(1).to_broadcast([128, DT, tc]),
            op=OP.mult))
        return tok

    def emit_lnaff(src3, out3, tc, xb_tmp, sqb, rows, rows_bf, alpha, beta,
                   dep_toks, pm=pb[4], pv=pb[5], pmb=pb[6], prb=pb[7],
                   tmp3=None):
        c_ph = ccol(0.5)[0:1, :]
        p.wait_many("v", dep_toks)
        tcast = p.op("v", lambda e: e.tensor_copy(
            xb_tmp.rearrange("p k t -> p (k t)"), src3.rearrange("p k t -> p (k t)")))
        p.wait_many("s", dep_toks)
        tsq = p.op("s", lambda e: e.activation(
            out=sqb.rearrange("p k t -> p (k t)"),
            in_=src3.rearrange("p k t -> p (k t)"), func=AF.Square))
        p.wait("t", tcast)
        p.wait("t", tsq)
        for k in range(DT):
            p.op("t", lambda e, k=k: e.matmul(
                pm[0:1, 0:tc], sb_ones_col_bf, xb_tmp[:, k],
                start=(k == 0), stop=(k == DT - 1)))
        for k in range(DT):
            tmm = p.op("t", lambda e, k=k: e.matmul(
                pv[0:1, 0:tc], sb_ones_col_bf, sqb[:, k],
                start=(k == 0), stop=(k == DT - 1)))
        m_row = rows[:, 0:tc]
        m2_row = rows[:, tc:2 * tc]
        var_row = rows[:, 2 * tc:3 * tc]
        srow = rows[:, 3 * tc:4 * tc]
        rrow = rows[:, 4 * tc:5 * tc]
        p.wait("v", tmm)
        p.op("v", lambda e: e.tensor_scalar(
            out=m_row, in0=pm[0:1, 0:tc], scalar1=1.0 / D, scalar2=None,
            op0=OP.mult))
        p.op("v", lambda e: e.scalar_tensor_tensor(
            out=m2_row, in0=m_row, scalar=float(D) / (D - 1), in1=m_row,
            op0=OP.mult, op1=OP.mult))
        tvar = p.op("v", lambda e: e.scalar_tensor_tensor(
            out=var_row, in0=pv[0:1, 0:tc], scalar=1.0 / (D - 1),
            in1=m2_row, op0=OP.mult, op1=OP.subtract))
        p.wait("s", tvar)
        p.op("s", lambda e: e.activation(out=srow, in_=var_row, func=AF.Ln))
        tst = p.op("s", lambda e, s_=c_ph: e.activation(
            out=srow, in_=srow, func=AF.Exp, scale=s_))
        p.wait("v", tst)
        p.op("v", lambda e: e.tensor_scalar(
            out=srow, in0=srow, scalar1=1e-6, scalar2=None, op0=OP.add))
        trc = p.op("v", lambda e: e.reciprocal(out=rrow, in_=srow))
        p.wait("s", trc)
        tmb = p.op("s", lambda e: e.activation(
            out=rows_bf[:, 0:tc], in_=m_row, func=AF.Copy))
        trs = p.op("s", lambda e: e.activation(
            out=rows_bf[:, tc:2 * tc], in_=rrow, func=AF.Copy))
        p.wait("t", tmb)
        p.wait("t", trs)
        t1_ = p.op("t", lambda e: e.matmul(
            pmb[:, 0:tc], sb_ones_row_bf, rows_bf[:, 0:tc], start=True, stop=True))
        t2_ = p.op("t", lambda e: e.matmul(
            prb[:, 0:tc], sb_ones_row_bf, rows_bf[:, tc:2 * tc], start=True, stop=True))
        p.wait("v", t1_)
        p.wait("v", t2_)
        tgt = tmp3 if tmp3 is not None else src3
        p.op("v", lambda e: e.tensor_tensor(
            out=tgt, in0=src3,
            in1=pmb[:, 0:tc].unsqueeze(1).to_broadcast([128, DT, tc]),
            op=OP.subtract))
        p.op("v", lambda e: e.tensor_tensor(
            out=tgt, in0=tgt,
            in1=prb[:, 0:tc].unsqueeze(1).to_broadcast([128, DT, tc]),
            op=OP.mult))
        tok = None
        for mt in range(DT):
            tok = p.op("v", lambda e, mt=mt: e.tensor_scalar(
                out=out3[:, mt], in0=tgt[:, mt],
                scalar1=alpha[:, mt:mt + 1], scalar2=beta[:, mt:mt + 1],
                op0=OP.mult, op1=OP.add))
        return tok

    def _phase_m1a():
        ar.push()
        w_inp = ar.alloc([128, DT, 2 * DI], BF16, "w_inp")
        inp_T_r = di["in_proj_T"].rearrange("(k p) m -> k p m", p=128)
        tw = [p.dma("g", w_inp[:, k], inp_T_r[k], g="w_inp") for k in range(DT)]
        b_inp = ar.alloc([128, 2 * EI], F32, "b_inp")
        tb1 = p.dma("q", b_inp, di["in_proj_b"], g="m1c")
        w_conv = ar.alloc([128, EI, 4], F32, "w_conv")
        tb2 = p.dma("q", w_conv, di["conv_w"], g="m1c")
        b_conv = ar.alloc([128, EI], F32, "b_conv")
        tb3 = p.dma("q", b_conv, di["conv_b"], g="m1c")

        xch = ar.alloc([128, DT, TCM], F32, "m1a_x")
        xb_tmp = ar.alloc([128, DT, TCM], BF16, "m1a_xb")
        sqb = ar.alloc([128, DT, TCM], BF16, "m1a_sqb")
        xnb = ar.alloc([128, DT, TCM], BF16, "m1a_xnb")
        xn1f = ar.alloc([128, DT, TCM], F32, "m1a_xn1f")
        rows = ar.alloc([1, 6 * TCM], F32, "m1a_rows")
        rows_bf = ar.alloc([1, 2 * TCM], BF16, "m1a_rowsbf")
        ina_sb = ar.alloc([128, DT], F32, "m1a_ina")
        tia = p.dma("q", ina_sb, di["in_norm_alpha"], g="m1c")
        inb_sb = ar.alloc([128, DT], F32, "m1a_inb")
        tib = p.dma("q", inb_sb, di["in_norm_bias"], g="m1c")
        xin = ar.alloc([128, EI, 3 + TCM], F32, "m1a_xin")
        cacc = [ar.alloc([128, TCM], F32, f"m1a_cacc{i}") for i in range(2)]
        xcf = [ar.alloc([128, TCM], F32, f"m1a_xcf{i}") for i in range(2)]
        xcb = [ar.alloc([128, TCM], BF16, f"m1a_xcb{i}") for i in range(2)]
        szf = [ar.alloc([128, TCM], F32, f"m1a_szf{i}") for i in range(2)]

        p.wait_many("v", [tb1, tb2, tb3, tia, tib])
        tpad = p.op("v", lambda e: e.memset(xin[:, :, 0:3], 0.0))
        p.wait_many("s", [tib])
        prev_hist = tpad
        prev_stores = []
        sz_st = [None, None]
        xc_st = [None, None]
        xcb_st = [None, None]

        prev_tsp = None
        prev_inproj = None
        for c in range(NCM):
            t0 = c * TCM
            tx = p.dma("q", xch, di["x_fm"].rearrange(
                "(k p) t -> p k t", p=128)[:, :, t0:t0 + TCM], g="xch")
            p.wait("v", prev_tsp)
            p.wait("v", prev_inproj)
            tok_xn0 = emit_ln0(xch, xn1f, TCM, xb_tmp, sqb, rows, rows_bf, [tx])
            p.wait("g", tok_xn0)
            t_sp = p.dma("g", ds["r_xn1"].rearrange(
                "(k p) t -> p k t", p=128)[:, :, t0:t0 + TCM], xn1f, g="sp_xn1")
            prev_tsp = t_sp
            tok_xn = emit_lnaff(xn1f, xnb, TCM, xb_tmp, sqb, rows, rows_bf,
                                ina_sb, inb_sb, [tok_xn0], tmp3=xch)
            p.wait("t", tok_xn)
            p.wait_many("t", tw)
            for mt in range(2 * EI):
                bank = pb[mt % 4]
                for k in range(DT):
                    tmm = p.op("t", lambda e, mt=mt, k=k, bank=bank: e.matmul(
                        bank, w_inp[:, k, mt * 128:(mt + 1) * 128], xnb[:, k],
                        start=(k == 0), stop=(k == DT - 1)))
                prev_inproj = tmm
                p.wait("s", tmm)
                if mt < EI:
                    if c > 0:
                        p.wait("s", prev_hist)
                    tev = p.op("s", lambda e, mt=mt, bank=bank: e.activation(
                        out=xin[:, mt, 3:3 + TCM], in_=bank,
                        func=AF.Identity, bias=b_inp[:, mt:mt + 1]))
                else:
                    sbuf_i = (mt - EI) % 2
                    p.wait("s", sz_st[sbuf_i])
                    tev = p.op("s", lambda e, mt=mt, bank=bank, sbuf_i=sbuf_i:
                               e.activation(out=szf[sbuf_i], in_=bank,
                                            func=AF.Silu, bias=b_inp[:, mt:mt + 1]))
                    p.wait("q", tev)
                    sz_st[sbuf_i] = p.dma("q", ds["r_sz"].rearrange(
                        "(k p) t -> k p t", p=128)[mt - EI, :, t0:t0 + TCM],
                        szf[sbuf_i], g=f"sp_sz{sbuf_i}")
                p.wait("t", tev)
            t_evin = tev
            for et in range(EI):
                p.wait("v", t_evin if et == 0 else None)
                bi = et % 2
                p.op("v", lambda e, et=et: e.tensor_scalar(
                    out=cacc[0], in0=xin[:, et, 0:TCM],
                    scalar1=w_conv[:, et, 0:1], scalar2=None,
                    op0=OP.mult))
                for j in (1, 2):
                    p.op("v", lambda e, et=et, j=j: e.scalar_tensor_tensor(
                        out=cacc[j % 2], in0=xin[:, et, j:j + TCM],
                        scalar=w_conv[:, et, j:j + 1], in1=cacc[(j + 1) % 2],
                        op0=OP.mult, op1=OP.add))
                tv = p.op("v", lambda e, et=et: e.scalar_tensor_tensor(
                    out=cacc[1], in0=xin[:, et, 3:3 + TCM],
                    scalar=w_conv[:, et, 3:4], in1=cacc[0],
                    op0=OP.mult, op1=OP.add))
                p.wait("s", tv)
                p.wait("s", xc_st[bi])
                ts = p.op("s", lambda e, et=et, bi=bi: e.activation(
                    out=xcf[bi], in_=cacc[1], func=AF.Silu,
                    bias=b_conv[:, et:et + 1]))
                p.wait("v", ts)
                p.wait("v", xcb_st[bi])
                tvv = p.op("v", lambda e, et=et, bi=bi: e.tensor_copy(
                    xcb[bi], xcf[bi]))
                p.wait("q", ts)
                xc_st[bi] = p.dma("q", ds["r_xc"].rearrange(
                    "(k p) t -> k p t", p=128)[et, :, t0:t0 + TCM], xcf[bi],
                    g=f"sp_xc{bi}")
                p.wait("q", tvv)
                xcb_st[bi] = p.dma("q", ds["r_xcb"].rearrange(
                    "(k p) t -> k p t", p=128)[et, :, t0:t0 + TCM], xcb[bi],
                    g=f"sp_xcb{bi}")
            prev_hist = p.op("v", lambda e: e.tensor_copy(
                xin[:, :, 0:3], xin[:, :, TCM:TCM + 3]))
            prev_stores = [xc_st[0], xc_st[1], xcb_st[0], xcb_st[1],
                           sz_st[0], sz_st[1], t_sp]
            p.wait("q", tok_xn)
        p.wait_many("q", prev_stores)
        p.barrier()
        ar.pop()

    if "M1a" in phases:
        _phase_m1a()

    def _phase_m1b():
        c_m1 = ccol(-1.0)
        c_p1 = ccol(1.0)
        ar.push()
        w_dt = ar.alloc([128, EI, DI], BF16, "w_dt")
        dt_T_r = di["dt_proj_T"].rearrange("(k p) m -> k p m", p=128)
        twd = [p.dma("g", w_dt[:, k], dt_T_r[k], g="w_dt") for k in range(EI)]
        w_xp = ar.alloc([128, EI, 2 * NS], BF16, "w_xp")
        xp_T_r = di["x_proj_T"].rearrange("(k p) m -> k p m", p=128)
        twx = [p.dma("g", w_xp[:, k], xp_T_r[k], g="w_xp") for k in range(EI)]
        b_dt = ar.alloc([128, EI], F32, "b_dt")
        tb1 = p.dma("q", b_dt, di["dt_proj_b"], g="m1bc")
        b_xp = ar.alloc([2 * NS, 1], F32, "b_xp")
        tb2 = p.dma("q", b_xp, di["x_proj_b"], g="m1bc")

        xcb = ar.alloc([128, EI, TCM], BF16, "m1b_xcb")
        xcf = ar.alloc([128, EI, TCM], F32, "m1b_xcf")
        rbuf = ar.alloc([128, TCM], F32, "m1b_r")
        abuf = ar.alloc([128, TCM], F32, "m1b_a")
        ebuf = ar.alloc([128, TCM], F32, "m1b_e")
        lbuf = ar.alloc([128, TCM], F32, "m1b_l")
        dtf = ar.alloc([128, TCM], F32, "m1b_dt")
        dtxb = ar.alloc([128, TCM], BF16, "m1b_dtx")
        bc_sb = ar.alloc([2 * NS, TCM], F32, "m1b_bc")

        p.wait_many("v", [tb1, tb2])
        prev_tst = None
        prev_tbc = None
        for c in range(NCM):
            t0 = c * TCM
            tx1 = p.dma("q", xcb, ds["r_xcb"].rearrange(
                "(k p) t -> p k t", p=128)[:, :, t0:t0 + TCM], g="xcb_l")
            tx2 = p.dma("q", xcf, ds["r_xc"].rearrange(
                "(k p) t -> p k t", p=128)[:, :, t0:t0 + TCM], g="xcf_l")
            p.wait("t", tx1)
            p.wait_many("t", twx if c == 0 else [])
            p.wait("t", prev_tbc)
            for k in range(EI):
                tmm = p.op("t", lambda e, k=k: e.matmul(
                    pb[6][0:2 * NS, 0:TCM], w_xp[:, k], xcb[:, k],
                    start=(k == 0), stop=(k == EI - 1)))
            p.wait("v", tmm)
            p.wait("v", prev_tst)
            tbc = p.op("v", lambda e: e.tensor_scalar(
                out=bc_sb, in0=pb[6][0:2 * NS, 0:TCM], scalar1=b_xp,
                scalar2=None, op0=OP.add))
            prev_tbc = tbc
            p.wait("q", tbc)
            tst = p.dma("q", ds["r_bc"][:, t0:t0 + TCM], bc_sb, g="sp_bc")
            prev_tst = tst
            p.wait_many("t", twd if c == 0 else [])
            for mt in range(EI):
                bank = pb[mt % 2]
                for k in range(EI):
                    tmm = p.op("t", lambda e, mt=mt, k=k, bank=bank: e.matmul(
                        bank, w_dt[:, k, mt * 128:(mt + 1) * 128], xcb[:, k],
                        start=(k == 0), stop=(k == EI - 1)))
                p.wait("s", tmm)
                trl = p.op("s", lambda e, mt=mt, bank=bank: e.activation(
                    out=rbuf, in_=bank, func=AF.Relu, bias=b_dt[:, mt:mt + 1]))
                tva = p.op("s", lambda e, mt=mt, bank=bank: e.activation(
                    out=abuf, in_=bank, func=AF.Abs, bias=b_dt[:, mt:mt + 1]))
                p.wait("t", tva)
                tse = p.op("s", lambda e, s_=c_m1: e.activation(
                    out=ebuf, in_=abuf, func=AF.Exp, scale=s_))
                tsl = p.op("s", lambda e, b=c_p1: e.activation(
                    out=lbuf, in_=ebuf, func=AF.Ln, bias=b))
                p.wait("v", tsl)
                p.wait("v", trl)
                p.op("v", lambda e: e.tensor_tensor(
                    out=dtf, in0=rbuf, in1=lbuf, op=OP.add))
                tdt = p.op("v", lambda e: e.tensor_scalar(
                    out=dtf, in0=dtf, scalar1=10.0, scalar2=1e-4,
                    op0=OP.min, op1=OP.max))
                p.wait("v", tx2 if mt == 0 else None)
                tdx = p.op("v", lambda e, mt=mt: e.tensor_tensor(
                    out=dtxb, in0=dtf, in1=xcf[:, mt], op=OP.mult))
                p.wait("q", tdt)
                ts1 = p.dma("q", ds["r_dt"].rearrange(
                    "(k p) t -> k p t", p=128)[mt, :, t0:t0 + TCM], dtf, g="sp_dt")
                p.wait("q", tdx)
                ts2 = p.dma("q", ds["r_dtx"].rearrange(
                    "(k p) t -> k p t", p=128)[mt, :, t0:t0 + TCM], dtxb, g="sp_dtx")
                p.wait("v", ts1)
                p.wait("v", ts2)
            p.wait("q", tmm)
            p.wait("q", tdx)
        p.barrier()
        ar.pop()

    if "M1b" in phases:
        _phase_m1b()

    def _phase_sw1():
        ar.push()
        A_sb = ar.alloc([128, EI, NS], F32, "A_sb")
        tA = p.dma("q", A_sb, di["A_log"], g="sc_A")
        p.wait("s", tA)
        tAe = p.op("s", lambda e: e.activation(
            out=A_sb.rearrange("p k n -> p (k n)"),
            in_=A_sb.rearrange("p k n -> p (k n)"), func=AF.Exp))
        p.wait("v", tAe)
        tAn = p.op("v", lambda e: e.tensor_scalar(
            out=A_sb.rearrange("p k n -> p (k n)"),
            in0=A_sb.rearrange("p k n -> p (k n)"),
            scalar1=-1.0, scalar2=None, op0=OP.mult))

        TC1 = 2048
        NC1 = T // TC1
        W1 = TC1 // WIN
        Bbc = ar.alloc([128, NS, TC1], BF16, "sw1_Bbc")
        dtc = ar.alloc([128, TC1], F32, "sw1_dt")
        dtxc = ar.alloc([128, TC1], BF16, "sw1_dtx")
        dAc = ar.alloc([128, TC1], BF16, "sw1_dA")
        dBxc = ar.alloc([128, TC1], BF16, "sw1_dBx")
        Fc = ar.alloc([128, TC1], BF16, "sw1_F")
        ucb = ar.alloc([128, TC1], F32, "sw1_uc")
        Pw = ar.alloc([128, W1], F32, "sw1_Pw")
        Pbuf = ar.alloc([128, NS, W1], BF16, "sw1_P")
        qbuf = ar.alloc([128, NS, W1], BF16, "sw1_q")

        tAn_done = tAn
        pst = {"P": None, "q": None}
        prev_cons = {"dt": None, "dtx": None, "B": None}
        prev_tF = None
        tP_prev = None
        for c in range(NC1):
            t0 = c * TC1
            p.wait("g", prev_cons["B"])
            tB = [p.dma("g", Bbc[:, n],
                        ds["r_bc"][n:n + 1, t0:t0 + TC1].to_broadcast([128, TC1]),
                        g="sw1_B") for n in range(NS)]
            for et in range(EI):
                p.wait("q", prev_cons["dt"])
                tdt = p.dma("q", dtc, ds["r_dt"].rearrange(
                    "(k p) t -> k p t", p=128)[et, :, t0:t0 + TC1], g="sw1_dt")
                p.wait("q", prev_cons["dtx"])
                tdx = p.dma("q", dtxc, ds["r_dtx"].rearrange(
                    "(k p) t -> k p t", p=128)[et, :, t0:t0 + TC1], g="sw1_dtx")
                for n in range(NS):
                    p.wait("s", tdt)
                    p.wait("s", tAn_done)
                    p.wait("s", prev_tF)
                    tse = p.op("s", lambda e, et=et, n=n: e.activation(
                        out=dAc, in_=dtc, func=AF.Exp,
                        scale=A_sb[:, et, n:n + 1]))
                    p.wait("v", tse)
                    if n == 0:
                        p.wait("s", pst["P"])
                        p.wait("v", pst["q"])
                    tcl = p.op("v", lambda e: e.tensor_scalar(
                        out=dAc, in0=dAc, scalar1=EXP_NEG10, scalar2=None,
                        op0=OP.max))
                    p.wait("v", tP_prev)
                    p.op("v", lambda e, et=et, n=n: e.tensor_scalar(
                        out=ucb, in0=dtc, scalar1=A_sb[:, et, n:n + 1],
                        scalar2=-10.0, op0=OP.mult, op1=OP.max))
                    tPr = p.op("v", lambda e: e.tensor_reduce(
                        out=Pw.unsqueeze(2),
                        in_=ucb.rearrange("p (w i) -> p w i", i=WIN),
                        axis=AX.X, op=OP.add))
                    p.wait("s", tPr)
                    tP = p.op("s", lambda e, n=n: e.activation(
                        out=Pbuf[:, n], in_=Pw, func=AF.Exp))
                    tP_prev = tP
                    p.wait("v", tdx)
                    p.wait("v", tB[-1])
                    tdb = p.op("v", lambda e, n=n: e.tensor_tensor(
                        out=dBxc, in0=dtxc, in1=Bbc[:, n], op=OP.mult))
                    p.op("v", lambda e: e.memset(
                        dAc.rearrange("p (w i) -> p w i", i=WIN)[:, :, 0:1], 0.0))
                    tF = p.op("v", lambda e: e.tensor_tensor_scan(
                        out=Fc, data0=dAc, data1=dBxc, initial=0.0,
                        op0=OP.mult, op1=OP.add))
                    prev_tF = tF
                    tq = p.op("v", lambda e, n=n: e.tensor_copy(
                        qbuf[:, n],
                        Fc.rearrange("p (w i) -> p w i", i=WIN)[:, :, WIN - 1]))
                p.wait("q", tP)
                prev_tF = (prev_tF, tPr)[0] if False else prev_tF
                p.wait("q", pst["P"])
                pst["P"] = p.dma("q", ds["r_P"].rearrange(
                    "(e n) p w -> p e n w", n=NS)[:, et, :, c * W1:(c + 1) * W1],
                    Pbuf, g="sw1_Pst")
                p.wait("q", tq)
                p.wait("q", pst["q"])
                pst["q"] = p.dma("q", ds["r_q"].rearrange(
                    "(e n) p w -> p e n w", n=NS)[:, et, :, c * W1:(c + 1) * W1],
                    qbuf, g="sw1_qst")
                prev_cons["dt"] = tse
                prev_cons["dtx"] = tdb
            prev_cons["B"] = tdb
        p.barrier()
        ar.pop()

    def _phase_l2():
        ar.push()
        c_mhalf = ccol(-0.5)
        c_ln10 = ccol(LN10)
        WC = 64
        NWC = NW // WC
        SIG = EI * NS
        P_l = ar.alloc([128, SIG, WC], BF16, "l2_P")
        q_l = ar.alloc([128, SIG, WC], BF16, "l2_q")
        Hob = ar.alloc([128, SIG, WC], BF16, "l2_Ho")
        Hcur = ar.alloc([128, SIG], F32, "l2_H")
        hu = ar.alloc([128, SIG], F32, "l2_hu")
        sq = ar.alloc([128, SIG], F32, "l2_sq")
        hn2 = ar.alloc([128, EI], F32, "l2_hn2")
        lnb = ar.alloc([128, EI], F32, "l2_ln")
        sE = ar.alloc([128, EI], F32, "l2_s")

        tz = p.op("v", lambda e: e.memset(Hcur, 0.0))
        hst = None
        tHm = None
        for wc in range(NWC):
            w0 = wc * WC
            tP = p.dma("q", P_l, ds["r_P"].rearrange(
                "s p w -> p s w")[:, :, w0:w0 + WC], g="l2_P")
            tq = p.dma("q", q_l, ds["r_q"].rearrange(
                "s p w -> p s w")[:, :, w0:w0 + WC], g="l2_q")
            p.wait("v", tP)
            p.wait("v", tq)
            p.wait("v", hst)
            for wi in range(WC):
                p.op("v", lambda e, wi=wi: e.tensor_tensor(
                    out=hu, in0=P_l[:, :, wi], in1=Hcur, op=OP.mult))
                p.op("v", lambda e, wi=wi: e.tensor_tensor(
                    out=hu, in0=hu, in1=q_l[:, :, wi], op=OP.add))
                p.op("v", lambda e: e.tensor_tensor(
                    out=sq, in0=hu, in1=hu, op=OP.mult))
                tred = p.op("v", lambda e: e.tensor_reduce(
                    out=hn2.unsqueeze(2),
                    in_=sq.rearrange("p (e n) -> p e n", n=NS),
                    axis=AX.X, op=OP.add))
                p.wait("s", tred)
                p.wait("s", tHm)
                p.op("s", lambda e: e.activation(out=lnb, in_=hn2, func=AF.Ln))
                tsE = p.op("s", lambda e, s_=c_mhalf, b=c_ln10: e.activation(
                    out=sE, in_=lnb, func=AF.Exp, scale=s_, bias=b))
                p.wait("v", tsE)
                p.op("v", lambda e: e.tensor_scalar(
                    out=sE, in0=sE, scalar1=1.0, scalar2=None, op0=OP.min))
                tHm = p.op("v", lambda e: e.tensor_tensor(
                    out=Hcur.rearrange("p (e n) -> p e n", n=NS).transpose([0, 2, 1]),
                    in0=hu.rearrange("p (e n) -> p e n", n=NS).transpose([0, 2, 1]),
                    in1=sE.unsqueeze(1).to_broadcast([128, NS, EI]),
                    op=OP.mult))
                th = p.op("v", lambda e, wi=wi: e.tensor_copy(
                    Hob[:, :, wi], Hcur))
            p.wait("q", th)
            hst = p.dma("q", ds["r_H"].rearrange(
                "s p w -> p s w")[:, :, w0:w0 + WC], Hob, g="l2_Hst")
        p.barrier()
        ar.pop()

    def _phase_sw2():
        ar.push()
        A_sb = ar.alloc([128, EI, NS], F32, "A_sb2")
        tA = p.dma("q", A_sb, di["A_log"], g="sc_A2")
        p.wait("s", tA)
        tAe = p.op("s", lambda e: e.activation(
            out=A_sb.rearrange("p k n -> p (k n)"),
            in_=A_sb.rearrange("p k n -> p (k n)"), func=AF.Exp))
        p.wait("v", tAe)
        tAn = p.op("v", lambda e: e.tensor_scalar(
            out=A_sb.rearrange("p k n -> p (k n)"),
            in0=A_sb.rearrange("p k n -> p (k n)"),
            scalar1=-1.0, scalar2=None, op0=OP.mult))
        Dp = ar.alloc([128, EI], F32, "sw2_D")
        tD = p.dma("q", Dp, di["D_param"], g="sw2_c")
        w_out = ar.alloc([128, EI, D], BF16, "w_out")
        out_T_r = di["out_proj_T"].rearrange("(k p) m -> k p m", p=128)
        two = [p.dma("g", w_out[:, k], out_T_r[k], g="w_out") for k in range(EI)]
        b_out = ar.alloc([128, DT], F32, "b_out")
        tbo = p.dma("q", b_out, di["out_proj_b"], g="sw2_c")
        al_sb = ar.alloc([128, DT], F32, "sw2_al")
        tal = p.dma("q", al_sb, di["norm_alpha"], g="sw2_c")
        be_sb = ar.alloc([128, DT], F32, "sw2_be")
        tbe = p.dma("q", be_sb, di["norm_bias"], g="sw2_c")

        TC2 = 512
        NC2 = T // TC2
        W2 = TC2 // WIN
        Bbc = ar.alloc([128, NS, TC2], BF16, "sw2_Bbc")
        Cbc = ar.alloc([128, NS, TC2], BF16, "sw2_Cbc")
        dtc = ar.alloc([128, TC2], F32, "sw2_dt")
        dtxc = ar.alloc([128, TC2], BF16, "sw2_dtx")
        xcfc = ar.alloc([128, TC2], F32, "sw2_xcf")
        szfc = ar.alloc([128, TC2], F32, "sw2_szf")
        dAc = ar.alloc([128, TC2], BF16, "sw2_dA")
        dBxc = ar.alloc([128, TC2], BF16, "sw2_dBx")
        hbuf = ar.alloc([128, TC2], BF16, "sw2_h")
        inj = ar.alloc([128, W2], F32, "sw2_inj")
        H_l = ar.alloc([128, NS, W2 + 1], BF16, "sw2_H")
        hCb = ar.alloc([128, NS, TC2], BF16, "sw2_hC")
        ygb = ar.alloc([128, EI, TC2], BF16, "sw2_yg")
        y2 = ar.alloc([128, TC2], F32, "sw2_y2")
        t1b = ar.alloc([128, DT, TC2], F32, "m3_t1")
        t1bf = ar.alloc([128, DT, TC2], BF16, "m3_t1b")
        sqb3 = ar.alloc([128, DT, TC2], BF16, "m3_sqb")
        xn1_l = ar.alloc([128, DT, TC2], BF16, "m3_xn1")
        xch2 = ar.alloc([128, DT, TC2], F32, "m3_x")
        rows = ar.alloc([1, 6 * TC2], F32, "m3_rows")
        rows_bf = ar.alloc([1, 2 * TC2], BF16, "m3_rowsbf")

        p.wait_many("v", [tD, tbo, tal, tbe])
        prev = {}
        prev_scan = None
        tmy_prev = None
        for c in range(NC2):
            t0 = c * TC2
            w0 = c * W2
            p.wait("g", prev.get("B"))
            tB = [p.dma("g", Bbc[:, n],
                        ds["r_bc"][n:n + 1, t0:t0 + TC2].to_broadcast([128, TC2]),
                        g="sw2_B") for n in range(NS)]
            p.wait("g", prev.get("C"))
            tC = [p.dma("g", Cbc[:, n],
                        ds["r_bc"][NS + n:NS + n + 1, t0:t0 + TC2].to_broadcast([128, TC2]),
                        g="sw2_C") for n in range(NS)]
            tyg_all = []
            for et in range(EI):
                p.wait("q", prev.get("dt"))
                tdt = p.dma("q", dtc, ds["r_dt"].rearrange(
                    "(k p) t -> k p t", p=128)[et, :, t0:t0 + TC2], g="sw2_dt")
                p.wait("q", prev.get("dtx"))
                tdx = p.dma("q", dtxc, ds["r_dtx"].rearrange(
                    "(k p) t -> k p t", p=128)[et, :, t0:t0 + TC2], g="sw2_dtx")
                p.wait("q", prev.get("xcf"))
                txc = p.dma("q", xcfc, ds["r_xc"].rearrange(
                    "(k p) t -> k p t", p=128)[et, :, t0:t0 + TC2], g="sw2_xc")
                p.wait("q", prev.get("szf"))
                tsz = p.dma("q", szfc, ds["r_sz"].rearrange(
                    "(k p) t -> k p t", p=128)[et, :, t0:t0 + TC2], g="sw2_sz")
                p.wait("q", prev.get("H"))
                if c == 0:
                    tH = p.dma("q", H_l[:, :, 1:W2 + 1], ds["r_H"].rearrange(
                        "(e n) p w -> p e n w", n=NS)[:, et, :, 0:W2], g="sw2_H")
                else:
                    tH = p.dma("q", H_l, ds["r_H"].rearrange(
                        "(e n) p w -> p e n w", n=NS)[:, et, :, w0 - 1:w0 + W2],
                        g="sw2_H")
                thc_last = None
                p.wait("v", tmy_prev)
                for n in range(NS):
                    p.wait("s", tdt)
                    p.wait("s", tAn)
                    p.wait("s", prev_scan)
                    tse = p.op("s", lambda e, et=et, n=n: e.activation(
                        out=dAc, in_=dtc, func=AF.Exp,
                        scale=A_sb[:, et, n:n + 1]))
                    p.wait("v", tse)
                    tcl = p.op("v", lambda e: e.tensor_scalar(
                        out=dAc, in0=dAc, scalar1=EXP_NEG10, scalar2=None,
                        op0=OP.max))
                    p.wait("v", tdx)
                    p.wait("v", tB[-1])
                    p.op("v", lambda e, n=n: e.tensor_tensor(
                        out=dBxc, in0=dtxc, in1=Bbc[:, n], op=OP.mult))
                    p.wait("v", tH)
                    dA_w = dAc.rearrange("p (w i) -> p w i", i=WIN)[:, :, 0]
                    dBx_w = dBxc.rearrange("p (w i) -> p w i", i=WIN)[:, :, 0]
                    if noinj:
                        pass
                    elif c == 0:
                        p.op("v", lambda e, n=n: e.tensor_tensor(
                            out=inj[:, 1:W2], in0=dA_w[:, 1:W2],
                            in1=H_l[:, n, 1:W2], op=OP.mult))
                        p.op("v", lambda e: e.tensor_tensor(
                            out=dBx_w[:, 1:W2], in0=dBx_w[:, 1:W2],
                            in1=inj[:, 1:W2], op=OP.add))
                    else:
                        p.op("v", lambda e, n=n: e.tensor_tensor(
                            out=inj, in0=dA_w, in1=H_l[:, n, 0:W2], op=OP.mult))
                        p.op("v", lambda e: e.tensor_tensor(
                            out=dBx_w, in0=dBx_w, in1=inj, op=OP.add))
                    p.op("v", lambda e: e.memset(
                        dAc.rearrange("p (w i) -> p w i", i=WIN)[:, :, 0:1], 0.0))
                    prev_scan = p.op("v", lambda e: e.tensor_tensor_scan(
                        out=hbuf, data0=dAc, data1=dBxc, initial=0.0,
                        op0=OP.mult, op1=OP.add))
                    if not noinj:
                        p.op("v", lambda e, n=n, c=c: e.tensor_copy(
                            hbuf.rearrange("p (w i) -> p w i", i=WIN)[:, :, WIN - 1],
                            H_l[:, n, (1 if c == 0 else 1):W2 + 1]))
                    p.wait("v", tC[-1])
                    thc = p.op("v", lambda e, n=n: e.tensor_tensor(
                        out=hCb[:, n], in0=hbuf, in1=Cbc[:, n], op=OP.mult))
                    thc_last = thc
                    if dbg and n == 0:
                        p.wait("q", thc)
                        p.dma("q", ds["r_dh"].rearrange(
                            "(k p) t -> k p t", p=128)[et, :, t0:t0 + TC2],
                            hbuf, g="dbg_h")
                p.wait("t", thc_last)
                bank = pb[c % 2]
                for n in range(NS):
                    tmy = p.op("t", lambda e, n=n, bank=bank: e.matmul(
                        bank, sb_ident_bf, hCb[:, n],
                        start=(n == 0), stop=(n == NS - 1)))
                tmy_prev = tmy
                p.wait("v", tmy)
                p.wait("v", txc)
                p.op("v", lambda e, et=et, bank=bank: e.scalar_tensor_tensor(
                    out=y2, in0=xcfc, scalar=Dp[:, et:et + 1], in1=bank,
                    op0=OP.mult, op1=OP.add))
                p.wait("v", tsz)
                tyg = p.op("v", lambda e, et=et: e.tensor_tensor(
                    out=ygb[:, et], in0=y2, in1=szfc, op=OP.mult))
                tyg_all.append(tyg)
                if dbg:
                    p.wait("q", tyg)
                    p.dma("q", ds["r_dyg"].rearrange(
                        "(k p) t -> k p t", p=128)[et, :, t0:t0 + TC2],
                        ygb[:, et], g="dbg_yg")
                prev["dt"] = tse
                prev["dtx"] = tdb if False else thc_last
                prev["xcf"] = tyg
                prev["szf"] = tyg
                prev["H"] = thc_last
                p.wait("t", tyg)
            prev["B"] = thc_last
            prev["C"] = thc_last
            prev["dtx"] = thc_last
            p.wait_many("t", two if c == 0 else [])
            p.wait("q", prev.get("xn1"))
            tld = p.dma("q", xn1_l, ds["r_xn1"].rearrange(
                "(k p) t -> p k t", p=128)[:, :, t0:t0 + TC2], g="m3_xn1")
            tld2 = p.dma("q", xch2, di["x_fm"].rearrange(
                "(k p) t -> p k t", p=128)[:, :, t0:t0 + TC2], g="m3_x")
            for mt in range(DT):
                bank = pb[2 + mt % 2]
                for k in range(EI):
                    tmo = p.op("t", lambda e, mt=mt, k=k, bank=bank: e.matmul(
                        bank, w_out[:, k, mt * 128:(mt + 1) * 128], ygb[:, k],
                        start=(k == 0), stop=(k == EI - 1)))
                p.wait("v", tmo)
                p.wait("v", tld)
                tt1 = p.op("v", lambda e, mt=mt, bank=bank: e.scalar_tensor_tensor(
                    out=t1b[:, mt], in0=bank, scalar=b_out[:, mt:mt + 1],
                    in1=xn1_l[:, mt], op0=OP.add, op1=OP.add))
                p.wait("t", tt1)
            prev["xn1"] = tt1
            if dbg:
                p.wait("q", tt1)
                p.dma("q", ds["r_dt1"].rearrange(
                    "(k p) t -> p k t", p=128)[:, :, t0:t0 + TC2], t1b,
                    g="dbg_t1")
            tcb = p.op("v", lambda e: e.tensor_copy(
                t1bf.rearrange("p k t -> p (k t)"),
                t1b.rearrange("p k t -> p (k t)")))
            p.wait("s", tt1)
            tsq = p.op("s", lambda e: e.activation(
                out=sqb3.rearrange("p k t -> p (k t)"),
                in_=t1b.rearrange("p k t -> p (k t)"), func=AF.Square))
            p.wait("t", tcb)
            p.wait("t", tsq)
            for k in range(DT):
                p.op("t", lambda e, k=k: e.matmul(
                    pb[4][0:1, 0:TC2], sb_ones_col_bf, t1bf[:, k],
                    start=(k == 0), stop=(k == DT - 1)))
            for k in range(DT):
                tmv = p.op("t", lambda e, k=k: e.matmul(
                    pb[5][0:1, 0:TC2], sb_ones_col_bf, sqb3[:, k],
                    start=(k == 0), stop=(k == DT - 1)))
            m_row = rows[:, 0:TC2]
            m2_row = rows[:, TC2:2 * TC2]
            var_row = rows[:, 2 * TC2:3 * TC2]
            srow = rows[:, 3 * TC2:4 * TC2]
            p.wait("v", tmv)
            p.op("v", lambda e: e.tensor_scalar(
                out=m_row, in0=pb[4][0:1, 0:TC2], scalar1=1.0 / D, scalar2=None,
                op0=OP.mult))
            p.op("v", lambda e: e.scalar_tensor_tensor(
                out=m2_row, in0=m_row, scalar=float(D) / (D - 1), in1=m_row,
                op0=OP.mult, op1=OP.mult))
            tvar = p.op("v", lambda e: e.scalar_tensor_tensor(
                out=var_row, in0=pb[5][0:1, 0:TC2], scalar=1.0 / (D - 1),
                in1=m2_row, op0=OP.mult, op1=OP.subtract))
            p.wait("s", tvar)
            c_ph = ccol(0.5)[0:1, :]
            p.op("s", lambda e: e.activation(
                out=srow, in_=var_row, func=AF.Ln))
            tst = p.op("s", lambda e, s_=c_ph: e.activation(
                out=srow, in_=srow, func=AF.Exp, scale=s_))
            p.wait("v", tst)
            p.op("v", lambda e: e.tensor_scalar(
                out=srow, in0=srow, scalar1=1e-6, scalar2=None, op0=OP.add))
            trc = p.op("v", lambda e: e.reciprocal(
                out=rows[:, 4 * TC2:5 * TC2], in_=srow))
            p.wait("s", trc)
            tmb = p.op("s", lambda e: e.activation(
                out=rows_bf[:, 0:TC2], in_=m_row, func=AF.Copy))
            trs = p.op("s", lambda e: e.activation(
                out=rows_bf[:, TC2:2 * TC2], in_=rows[:, 4 * TC2:5 * TC2],
                func=AF.Copy))
            p.wait("t", tmb)
            p.wait("t", trs)
            t1_ = p.op("t", lambda e: e.matmul(
                pb[6][:, 0:TC2], sb_ones_row_bf, rows_bf[:, 0:TC2],
                start=True, stop=True))
            t2_ = p.op("t", lambda e: e.matmul(
                pb[7][:, 0:TC2], sb_ones_row_bf, rows_bf[:, TC2:2 * TC2],
                start=True, stop=True))
            p.wait("v", t1_)
            p.wait("v", t2_)
            p.op("v", lambda e: e.tensor_tensor(
                out=t1b, in0=t1b,
                in1=pb[6][:, 0:TC2].unsqueeze(1).to_broadcast([128, DT, TC2]),
                op=OP.subtract))
            p.op("v", lambda e: e.tensor_tensor(
                out=t1b, in0=t1b,
                in1=pb[7][:, 0:TC2].unsqueeze(1).to_broadcast([128, DT, TC2]),
                op=OP.mult))
            p.wait("v", tld2)
            for mt in range(DT):
                p.op("v", lambda e, mt=mt: e.tensor_scalar(
                    out=t1b[:, mt], in0=t1b[:, mt],
                    scalar1=al_sb[:, mt:mt + 1], scalar2=be_sb[:, mt:mt + 1],
                    op0=OP.mult, op1=OP.add))
            tfin = p.op("v", lambda e: e.tensor_tensor(
                out=xch2, in0=xch2, in1=t1b, op=OP.add))
            p.wait("q", tfin)
            tres = p.dma("q", ds["r_res"].rearrange(
                "(k p) t -> p k t", p=128)[:, :, t0:t0 + TC2], xch2, g="m3_res")
            prev["x2"] = tfin
            p.wait("v", tmo)
            p.wait("q", tres)
        p.barrier()
        ar.pop()

    if "SCAN" in phases:
        _phase_sw1()
        _phase_l2()
        _phase_sw2()

    def _phase_ffn(tag):
        ar.push()
        TCF = 512
        NCF = T // TCF
        xch = ar.alloc([128, DT, TCF], F32, f"{tag}_x")
        xb_tmp = ar.alloc([128, DT, TCF], BF16, f"{tag}_xb")
        sqb = ar.alloc([128, DT, TCF], BF16, f"{tag}_sqb")
        xnb = ar.alloc([128, DT, TCF], F32, f"{tag}_xnb")
        rows = ar.alloc([1, 6 * TCF], F32, f"{tag}_rows")
        rows_bf = ar.alloc([1, 2 * TCF], BF16, f"{tag}_rowsbf")
        prev_sp = None
        prev_xn = None
        for c in range(NCF):
            t0 = c * TCF
            p.wait("q", prev_xn)
            tx = p.dma("q", xch, ds["r_res"].rearrange(
                "(k p) t -> p k t", p=128)[:, :, t0:t0 + TCF], g=f"{tag}_xl")
            p.wait("v", prev_sp)
            tok_xn = emit_ln0(xch, xnb, TCF, xb_tmp, sqb, rows, rows_bf, [tx])
            prev_xn = tok_xn
            p.wait("q", tok_xn)
            prev_sp = p.dma("q", ds["r_xnf"].rearrange(
                "(k p) t -> p k t", p=128)[:, :, t0:t0 + TCF], xnb, g=f"{tag}_xns")
        p.barrier()
        ar.pop()

        ar.push()
        NTT = T // 128
        w_r = ar.alloc([128, DT, NE], F32, f"{tag}_wr")
        trw = [p.dma("q", w_r[:, k], di["router_T"].rearrange(
            "(k p) m -> k p m", p=128)[k], g=f"{tag}_rw") for k in range(DT)]
        rb_b = ar.alloc([1, NE], BF16, f"{tag}_rbb")
        trb = p.dma("g", rb_b, di["router_b"], g=f"{tag}_rwb")
        rb_bc = ar.alloc([128, NE], F32, f"{tag}_rbc")
        xnt = ar.alloc([128, DT, 128], F32, f"{tag}_xnt")
        probs = ar.alloc([128, NE], F32, f"{tag}_probs")
        top8 = ar.alloc([128, 8], F32, f"{tag}_top8")
        mrep = ar.alloc([128, NE], F32, f"{tag}_mrep")
        gsc = ar.alloc([128, 8], F32, f"{tag}_gsc")
        rgate = ar.alloc([128, NTT, NE], F32, f"{tag}_rgate")
        p.wait_many("t", trw + [trb])
        tbc = p.op("t", lambda e: e.matmul(
            pb[6][:, 0:NE], sb_ones_row_bf, rb_b, start=True, stop=True))
        p.wait("v", tbc)
        tbc2 = p.op("v", lambda e: e.tensor_copy(rb_bc, pb[6][:, 0:NE]))
        prev_x = None
        tprobs_prev = None
        for tt in range(NTT):
            p.wait("q", prev_x)
            txn = p.dma("q", xnt, ds["r_xnf"].rearrange(
                "(k p) t -> p k t", p=128)[:, :, tt * 128:(tt + 1) * 128],
                g=f"{tag}_xnt")
            p.wait("t", txn)
            p.wait("t", tprobs_prev)
            for k in range(DT):
                tmm = p.op("t", lambda e, k=k: e.matmul(
                    pb[7][:, 0:NE], xnt[:, k], w_r[:, k],
                    start=(k == 0), stop=(k == DT - 1)))
            p.wait("v", tmm)
            p.wait("v", tbc2)
            tprobs_prev = p.op("v", lambda e: e.tensor_tensor(
                out=probs, in0=pb[7][:, 0:NE], in1=rb_bc, op=OP.add))
            p.op("v", lambda e: e.tensor_reduce(
                out=gsc[:, 0:1], in_=probs, axis=AX.X, op=OP.max))
            tng = p.op("v", lambda e: e.tensor_scalar(
                out=gsc[:, 1:2], in0=gsc[:, 0:1], scalar1=-1.0, scalar2=None,
                op0=OP.mult))
            p.wait("s", tng)
            tex = p.op("s", lambda e: e.activation(
                out=probs, in_=probs, func=AF.Exp, bias=gsc[:, 1:2]))
            p.wait("v", tex)
            p.op("v", lambda e: e.max(out=top8, in_=probs))
            p.op("v", lambda e: e.tensor_scalar(
                out=mrep, in0=probs, scalar1=top8[:, 5:6], scalar2=None,
                op0=OP.subtract))
            p.op("v", lambda e: e.tensor_scalar(
                out=mrep, in0=mrep, scalar1=1e30, scalar2=1.0,
                op0=OP.mult, op1=OP.add))
            p.op("v", lambda e: e.tensor_scalar(
                out=mrep, in0=mrep, scalar1=0.0, scalar2=1.0,
                op0=OP.max, op1=OP.min))
            p.op("v", lambda e, tt=tt: e.tensor_tensor(
                out=rgate[:, tt], in0=mrep, in1=probs, op=OP.mult))
            p.op("v", lambda e, tt=tt: e.tensor_reduce(
                out=gsc[:, 2:3], in_=rgate[:, tt], axis=AX.X, op=OP.add))
            p.op("v", lambda e: e.reciprocal(out=gsc[:, 3:4], in_=gsc[:, 2:3]))
            p.op("v", lambda e, tt=tt: e.tensor_scalar(
                out=rgate[:, tt], in0=rgate[:, tt], scalar1=gsc[:, 3:4],
                scalar2=None, op0=OP.mult))
            p.op("v", lambda e, tt=tt: e.tensor_reduce(
                out=gsc[:, 4:5], in_=rgate[:, tt], axis=AX.X, op=OP.add))
            p.op("v", lambda e: e.reciprocal(out=gsc[:, 5:6], in_=gsc[:, 4:5]))
            tg = p.op("v", lambda e, tt=tt: e.tensor_scalar(
                out=rgate[:, tt], in0=rgate[:, tt], scalar1=gsc[:, 5:6],
                scalar2=None, op0=OP.mult))
            prev_x = tg
            if dbg and tag == "f1":
                p.wait("q", tg)
                p.dma("q", ds["r_drg"][tt], rgate[:, tt], g="dbg_rg")
                p.dma("q", ds["r_dpr"][tt], probs, g="dbg_pr")
                p.dma("q", ds["r_dt8"][tt], top8, g="dbg_t8")
        p.barrier()

        NH = 2
        DH = D // NH
        wq = ar.alloc([128, NE // 2, DT, DH], BF16, f"{tag}_wq")
        accb = ar.alloc([128, NTT, DH], F32, f"{tag}_acc")
        xnt2 = [ar.alloc([128, DT, 128], BF16, f"{tag}_xnt2{i}") for i in range(2)]
        ident_f32 = ar.alloc([128, 128], F32, f"{tag}_idf")
        resb = ar.alloc([128, 4, 128], F32, f"{tag}_resb")
        tid = p.dma("q", ident_f32, c_ident, g=f"{tag}_id")
        ew_r = di["expert_w"].rearrange("e (k p) m -> e k p m", p=128)
        prev_wq_tok = None
        prev_acc = None
        prev_x2 = [None, None]
        for hq in range(NH):
            for eb in range(2):
                p.wait("g", prev_wq_tok)
                twq = []
                for ei in range(8):
                    ex = eb * 8 + ei
                    twq.append(p.dma("g", wq[:, ei],
                               ew_r[ex, :, :, hq * DH:(hq + 1) * DH].rearrange(
                                   "k p m -> p k m"), g=f"{tag}_wq{eb}"))
                for tt in range(NTT):
                    bi = tt % 2
                    p.wait("g", prev_x2[bi])
                    txn = p.dma("g", xnt2[bi], ds["r_xnf"].rearrange(
                        "(k p) t -> p k t", p=128)[:, :, tt * 128:(tt + 1) * 128],
                        g=f"{tag}_xt{bi}")
                    p.wait("t", txn)
                    p.wait_many("t", twq if tt == 0 else [])
                    for k in range(DT):
                        for ei in range(8):
                            tmm = p.op("t", lambda e, k=k, ei=ei, bi=bi: e.matmul(
                                pb[ei], xnt2[bi][:, k], wq[:, ei, k],
                                start=(k == 0), stop=(k == DT - 1)))
                    p.wait("v", tmm)
                    for ei in range(8):
                        ex = eb * 8 + ei
                        if eb == 0 and ei == 0:
                            tev = p.op("v", lambda e, ei=ei, ex=ex, tt=tt:
                                       e.tensor_scalar(
                                           out=accb[:, tt], in0=pb[ei],
                                           scalar1=rgate[:, tt, ex:ex + 1],
                                           scalar2=None, op0=OP.mult))
                        else:
                            tev = p.op("v", lambda e, ei=ei, ex=ex, tt=tt:
                                       e.scalar_tensor_tensor(
                                           out=accb[:, tt], in0=pb[ei],
                                           scalar=rgate[:, tt, ex:ex + 1],
                                           in1=accb[:, tt],
                                           op0=OP.mult, op1=OP.add))
                    p.wait("t", tev)
                    prev_x2[bi] = tmm
                prev_wq_tok = tmm
            p.wait_many("t", [tid] if hq == 0 else [])
            prev_rst = None
            last_ttr = None
            for tt in range(NTT):
                p.wait("q", prev_rst)
                trl = p.dma("q", resb, ds["r_res"].rearrange(
                    "(k p) t -> p k t", p=128)[:, hq * 4:hq * 4 + 4,
                                               tt * 128:(tt + 1) * 128],
                    g=f"{tag}_rl")
                tadds = []
                for j in range(4):
                    bank = pb[4 + j % 2]
                    p.wait("t", tadds[-1] if tadds else None)
                    ttr = p.op("t", lambda e, tt=tt, j=j, bank=bank: e.transpose(
                        bank[:, 0:128], accb[:, tt, j * 128:(j + 1) * 128],
                        ident_f32))
                    last_ttr = ttr
                    p.wait("v", ttr)
                    p.wait("v", trl)
                    tadd = p.op("v", lambda e, j=j, bank=bank: e.tensor_tensor(
                        out=resb[:, j], in0=resb[:, j], in1=bank[:, 0:128],
                        op=OP.add))
                    tadds.append(tadd)
                p.wait("q", tadd)
                prev_rst = p.dma("q", ds["r_res"].rearrange(
                    "(k p) t -> p k t", p=128)[:, hq * 4:hq * 4 + 4,
                                               tt * 128:(tt + 1) * 128],
                    resb, g=f"{tag}_rs")
            prev_acc = last_ttr
            p.wait("v", prev_acc)
            p.wait("t", tadd)
            p.wait("q", prev_rst)
        p.barrier()
        ar.pop()

    if "FFN1" in phases:
        _phase_ffn("f1")
    if "FFN2" in phases:
        _phase_ffn("f2")

    def _phase_attb():
        ar.push()
        TCA = 512
        NCA = T // TCA
        EPSB = 1e-6
        mask1 = ar.alloc([128, DT, H], BF16, "ab_m1")
        mask2 = ar.alloc([H, DT, 128], BF16, "ab_m2")
        tm1 = p.dma("g", mask1, c_m1, g="ab_c")
        tm2 = p.dma("g", mask2, c_m2, g="ab_c")
        ksum = ar.alloc([128, DT], F32, "ab_ksum")
        qsum = ar.alloc([128, DT], F32, "ab_qsum")
        qsisum = ar.alloc([128, DT], F32, "ab_qsisum")
        ksosum = ar.alloc([128, DT], F32, "ab_ksosum")
        kv_sb = ar.alloc([128, 512], F32, "ab_kv")
        zrec = ar.alloc([H, 4], F32, "ab_z")
        si_sb = ar.alloc([H, TCA], F32, "ab_si")
        sis_bf = ar.alloc([H, TCA], BF16, "ab_sisbf")
        sa_sb = ar.alloc([H, TCA], F32, "ab_sa")
        dsc("r_row", [2, DT, 128])
        bq_c = ar.alloc([128, DT], F32, "ab_bq")
        tb0 = p.dma("q", bq_c, di["bq"], g="ab_c2")
        bo_c = ar.alloc([128, DT], F32, "ab_bo")
        tb1 = p.dma("q", bo_c, di["bo"], g="ab_c2")
        bkr = ar.alloc([1, D], BF16, "ab_bkr")
        tb2 = p.dma("g", bkr, di["bk_row"], g="ab_c")
        bvr = ar.alloc([1, D], BF16, "ab_bvr")
        tb3 = p.dma("g", bvr, di["bv_row"], g="ab_c")
        bk_bc = ar.alloc([128, D], F32, "ab_bkbc")
        bv_bc = ar.alloc([128, D], F32, "ab_bvbc")
        p.wait_many("t", [tm1, tm2, tb2, tb3])
        tbb1 = p.op("t", lambda e: e.matmul(
            pb[6][:, 0:D // 2], sb_ones_row_bf, bkr[:, 0:D // 2],
            start=True, stop=True))
        tbb1b = p.op("t", lambda e: e.matmul(
            pb[7][:, 0:D // 2], sb_ones_row_bf, bkr[:, D // 2:],
            start=True, stop=True))
        p.wait("v", tbb1)
        p.wait("v", tbb1b)
        p.op("v", lambda e: e.tensor_copy(bk_bc[:, 0:D // 2], pb[6][:, 0:D // 2]))
        tcp1 = p.op("v", lambda e: e.tensor_copy(bk_bc[:, D // 2:], pb[7][:, 0:D // 2]))
        p.wait("t", tcp1)
        tbb2 = p.op("t", lambda e: e.matmul(
            pb[6][:, 0:D // 2], sb_ones_row_bf, bvr[:, 0:D // 2],
            start=True, stop=True))
        tbb2b = p.op("t", lambda e: e.matmul(
            pb[7][:, 0:D // 2], sb_ones_row_bf, bvr[:, D // 2:],
            start=True, stop=True))
        p.wait("v", tbb2)
        p.wait("v", tbb2b)
        p.op("v", lambda e: e.tensor_copy(bv_bc[:, 0:D // 2], pb[6][:, 0:D // 2]))
        tcp2 = p.op("v", lambda e: e.tensor_copy(bv_bc[:, D // 2:], pb[7][:, 0:D // 2]))
        tz1 = p.op("v", lambda e: e.memset(ksum, 0.0))
        p.op("v", lambda e: e.memset(qsum, 0.0))
        p.op("v", lambda e: e.memset(qsisum, 0.0))
        p.op("v", lambda e: e.memset(ksosum, 0.0))
        tz2 = p.op("v", lambda e: e.memset(kv_sb, 0.0))

        qfm = ar.alloc([128, DT, TCA], BF16, "ab_qfm")
        ub = ar.alloc([128, DT, TCA], BF16, "ab_ub")
        sum8 = ar.alloc([128, DT], F32, "ab_sum8")
        u2 = ar.alloc([128, D], F32, "ab_u2")
        ktile = ar.alloc([128, D], BF16, "ab_ktile")
        vtile = ar.alloc([128, D], BF16, "ab_vtile")
        so16 = ar.alloc([128, H], F32, "ab_so16")
        ez16 = ar.alloc([128, H], F32, "ab_ez16")
        ez_bf = ar.alloc([128, H], BF16, "ab_ezbf")
        idf = ar.alloc([128, 128], F32, "ab_idf")
        tidf = p.dma("q", idf, c_ident, g="ab_c2")
        row8 = ar.alloc([DT, 128], F32, "ab_row8")
        qse_bc = ar.alloc([128, D], BF16, "ab_qse")
        qsie_bc = ar.alloc([128, D], BF16, "ab_qsie")
        kso = ar.alloc([128, D], BF16, "ab_kso")
        kez = ar.alloc([128, D], BF16, "ab_kez")
        zpsum = ar.alloc([H, 4], F32, "ab_zp")

        ar.push()
        wqkv = ar.alloc([128, 3, DT, D], BF16, "ab_wqkv")
        tw = []
        for i, nm in enumerate(("Wq", "Wk", "Wv")):
            r = di[nm + "_T"].rearrange("(k p) m -> k p m", p=128)
            for k in range(DT):
                tw.append(p.dma("g", wqkv[:, i, k], r[k], g="ab_w"))
        xch = ar.alloc([128, DT, TCA], F32, "ab_x")
        xb_tmp = ar.alloc([128, DT, TCA], BF16, "ab_xb")
        sqb = ar.alloc([128, DT, TCA], BF16, "ab_sqb")
        xnb = ar.alloc([128, DT, TCA], BF16, "ab_xnb")
        rows = ar.alloc([1, 6 * TCA], F32, "ab_rows")
        rows_bf = ar.alloc([1, 2 * TCA], BF16, "ab_rowsbf")
        ktm = ar.alloc([128, 2, D], BF16, "ab_ktm")
        vtm = ar.alloc([128, 2, D], BF16, "ab_vtm")
        tmpf = ar.alloc([128, D], F32, "ab_tmpf")
        sum8 = ar.alloc([128, DT], F32, "ab_sum8")
        prev = {}
        for c in range(NCA):
            t0 = c * TCA
            p.wait("q", prev.get("x"))
            tx = p.dma("q", xch, ds["r_res"].rearrange(
                "(k p) t -> p k t", p=128)[:, :, t0:t0 + TCA], g="ab_xl")
            p.wait("v", prev.get("xnb"))
            tok_xn = emit_ln0(xch, xnb, TCA, xb_tmp, sqb, rows, rows_bf, [tx])
            prev["x"] = tok_xn
            p.wait_many("t", tw if c == 0 else [])
            p.wait("t", tok_xn)
            for mt in range(DT):
                bank = pb[mt % 2]
                for k in range(DT):
                    tmq = p.op("t", lambda e, mt=mt, k=k, bank=bank: e.matmul(
                        bank, wqkv[:, 0, k, mt * 128:(mt + 1) * 128], xnb[:, k],
                        start=(k == 0), stop=(k == DT - 1)))
                p.wait("s", tmq)
                p.wait("s", prev.get("qfm"))
                p.wait("s", prev.get("qfm_dma"))
                tsg = p.op("s", lambda e, mt=mt, bank=bank: e.activation(
                    out=qfm[:, mt], in_=bank, func=AF.Sigmoid,
                    bias=bq_c[:, mt:mt + 1]))
                p.wait("t", tsg)
            prev["qfm"] = tsg
            p.wait("v", tsg)
            tqr = p.op("v", lambda e: e.tensor_reduce(
                out=sum8.unsqueeze(2), in_=qfm, axis=AX.X, op=OP.add))
            tqa = p.op("v", lambda e: e.tensor_tensor(
                out=qsum, in0=qsum, in1=sum8, op=OP.add))
            p.wait("q", tsg)
            tqs = p.dma("q", ds["r_qb"].rearrange(
                "(k p) t -> p k t", p=128)[:, :, t0:t0 + TCA], qfm, g="ab_qs")
            prev["qfm"] = tqr
            prev["qfm_dma"] = tqs
            for sub in range(4):
                tt = c * 4 + sub
                bi = sub % 2
                for i, (dst, bias_bc, sig) in enumerate(
                        ((ktm, bk_bc, True), (vtm, bv_bc, False))):
                    for nslc in range(2):
                        bank = pb[2 + nslc]
                        for k in range(DT):
                            tmk = p.op("t", lambda e, i=i, k=k, sub=sub, nslc=nslc, bank=bank: e.matmul(
                                bank, xnb[:, k, sub * 128:(sub + 1) * 128],
                                wqkv[:, 1 + i, k, nslc * 512:(nslc + 1) * 512],
                                start=(k == 0), stop=(k == DT - 1)))
                        p.wait("v", tmk)
                        p.wait("v", prev.get(f"kv{i}{bi}"))
                        tva = p.op("v", lambda e, i=i, nslc=nslc, bank=bank, bias_bc=bias_bc: e.tensor_tensor(
                            out=tmpf[:, nslc * 512:(nslc + 1) * 512], in0=bank,
                            in1=bias_bc[:, nslc * 512:(nslc + 1) * 512], op=OP.add))
                        p.wait("t", tva)
                    if sig:
                        p.wait("s", tva)
                        tev = p.op("s", lambda e, bi=bi, dst=dst: e.activation(
                            out=dst[:, bi], in_=tmpf, func=AF.Sigmoid))
                        p.wait("v", tev)
                    else:
                        tev = p.op("v", lambda e, bi=bi, dst=dst: e.tensor_copy(
                            dst[:, bi], tmpf))
                    p.wait("q", tev)
                    tst = p.dma("q", (ds["r_kb"] if i == 0 else ds["r_vb"]).rearrange(
                        "(a p) d -> a p d", p=128)[tt], dst[:, bi], g=f"ab_kvs{i}{bi}")
                    prev[f"kv{i}{bi}"] = tst
                    if i == 0:
                        p.wait("t", tev)
                        for m8 in range(DT):
                            tks = p.op("t", lambda e, bi=bi, m8=m8: e.matmul(
                                pb[6][:, m8:m8 + 1],
                                ktm[:, bi, m8 * 128:(m8 + 1) * 128],
                                sb_ones_col_bf, start=True, stop=True))
                        p.wait("v", tks)
                        tka = p.op("v", lambda e: e.tensor_tensor(
                            out=ksum, in0=ksum, in1=pb[6][:, 0:DT], op=OP.add))
                        p.wait("t", tka)
        teps = p.op("v", lambda e: e.tensor_scalar(
            out=qsum, in0=qsum, scalar1=EPSB, scalar2=None, op0=OP.add))
        p.op("v", lambda e: e.tensor_scalar(
            out=ksum, in0=ksum, scalar1=EPSB, scalar2=None, op0=OP.add))
        p.wait("t", teps)
        p.wait("t", tidf)
        ttr = p.op("t", lambda e: e.transpose(pb[7][0:DT, 0:128], qsum, idf))
        p.wait("v", ttr)
        trow = p.op("v", lambda e: e.tensor_copy(row8, pb[7][0:DT, 0:128]))
        p.wait("q", trow)
        tqb = p.dma("q", ds["r_row"][0], row8, g="ab_bounce")
        p.barrier()
        ar.pop()

        tqe = p.dma("g", qse_bc, ds["r_row"][0].rearrange(
            "k p -> (k p)").unsqueeze(0).to_broadcast([128, D]), g="ab_qse")
        prevB = {}
        tsi_prev = None
        for c in range(NCA):
            t0 = c * TCA
            p.wait("q", prevB.get("q"))
            tq = p.dma("q", qfm, ds["r_qb"].rearrange(
                "(k p) t -> p k t", p=128)[:, :, t0:t0 + TCA], g="ab_ql")
            p.wait("v", tq)
            for k in range(DT):
                tu = p.op("v", lambda e, k=k: e.tensor_scalar(
                    out=ub[:, k], in0=qfm[:, k], scalar1=EPSB,
                    scalar2=ksum[:, k:k + 1], op0=OP.add, op1=OP.mult))
            p.wait("t", tu)
            p.wait("t", tsi_prev)
            for k in range(DT):
                tmsi = p.op("t", lambda e, k=k: e.matmul(
                    pb[4][0:H, 0:TCA], mask1[:, k], ub[:, k],
                    start=(k == 0), stop=(k == DT - 1)))
            p.wait("v", tmsi)
            tsi = p.op("v", lambda e: e.reciprocal(
                out=si_sb, in_=pb[4][0:H, 0:TCA]))
            tsi_prev = tsi
            tsib = p.op("v", lambda e: e.tensor_copy(sis_bf, si_sb))
            p.wait("t", tsib)
            for k in range(DT):
                tmsb = p.op("t", lambda e, k=k: e.matmul(
                    pb[5][:, 0:TCA], mask2[:, k], sis_bf,
                    start=True, stop=True))
                p.wait("v", tmsb)
                tqsi = p.op("v", lambda e, k=k: e.tensor_tensor(
                    out=ub[:, k], in0=qfm[:, k], in1=pb[5][:, 0:TCA], op=OP.mult))
                p.wait("t", tqsi)
            trq = p.op("v", lambda e: e.tensor_reduce(
                out=sum8.unsqueeze(2), in_=ub, axis=AX.X, op=OP.add))
            tacq = p.op("v", lambda e: e.tensor_tensor(
                out=qsisum, in0=qsisum, in1=sum8, op=OP.add))
            prevB["q"] = trq
            for sub in range(4):
                tt = c * 4 + sub
                p.wait("q", prevB.get("k"))
                tk = p.dma("q", ktile, ds["r_kb"].rearrange(
                    "(a p) d -> a p d", p=128)[tt], g="ab_kl")
                p.wait("v", tk)
                p.wait("v", tqe if (c == 0 and sub == 0) else None)
                p.op("v", lambda e: e.scalar_tensor_tensor(
                    out=u2, in0=ktile, scalar=EPSB, in1=qse_bc,
                    op0=OP.add, op1=OP.mult))
                p.op("v", lambda e: e.tensor_reduce(
                    out=so16.unsqueeze(2),
                    in_=u2.rearrange("p (h i) -> p h i", i=HD),
                    axis=AX.X, op=OP.add))
                p.op("v", lambda e: e.reciprocal(out=so16, in_=so16))
                tkso = p.op("v", lambda e: e.tensor_tensor(
                    out=kso.rearrange("p (h i) -> p h i", i=HD).transpose([0, 2, 1]),
                    in0=ktile.rearrange("p (h i) -> p h i", i=HD).transpose([0, 2, 1]),
                    in1=so16.unsqueeze(1).to_broadcast([128, HD, H]),
                    op=OP.mult))
                prevB["k"] = tkso
                p.wait("t", tkso)
                for m8 in range(DT):
                    tks = p.op("t", lambda e, m8=m8: e.matmul(
                        pb[6][:, m8:m8 + 1], kso[:, m8 * 128:(m8 + 1) * 128],
                        sb_ones_col_bf, start=True, stop=True))
                p.wait("v", tks)
                tka = p.op("v", lambda e: e.tensor_tensor(
                    out=ksosum, in0=ksosum, in1=pb[6][:, 0:DT], op=OP.add))
                p.wait("t", tka)
        t1e = p.op("v", lambda e: e.tensor_scalar(
            out=qsisum, in0=qsisum, scalar1=EPSB, scalar2=None, op0=OP.add))
        t2e = p.op("v", lambda e: e.tensor_scalar(
            out=ksosum, in0=ksosum, scalar1=EPSB, scalar2=None, op0=OP.add))
        p.wait("t", t1e)
        ttr2 = p.op("t", lambda e: e.transpose(pb[7][0:DT, 0:128], qsisum, idf))
        p.wait("v", ttr2)
        trow2 = p.op("v", lambda e: e.tensor_copy(row8, pb[7][0:DT, 0:128]))
        p.wait("q", trow2)
        tqb2 = p.dma("q", ds["r_row"][1], row8, g="ab_bounce")
        p.barrier()

        tqe2 = p.dma("g", qsie_bc, ds["r_row"][1].rearrange(
            "k p -> (k p)").unsqueeze(0).to_broadcast([128, D]), g="ab_qsie")
        prevC = {}
        tzm_prev = None
        tzz = p.op("v", lambda e: e.memset(zpsum, 0.0))
        for tt in range(T // 128):
            p.wait("q", prevC.get("k"))
            tk = p.dma("q", ktile, ds["r_kb"].rearrange(
                "(a p) d -> a p d", p=128)[tt], g="ab_kl2")
            p.wait("q", prevC.get("v"))
            tv = p.dma("q", vtile, ds["r_vb"].rearrange(
                "(a p) d -> a p d", p=128)[tt], g="ab_vl2")
            p.wait("v", tk)
            p.wait("v", tqe2 if tt == 0 else None)
            p.op("v", lambda e: e.scalar_tensor_tensor(
                out=u2, in0=ktile, scalar=EPSB, in1=qsie_bc,
                op0=OP.add, op1=OP.mult))
            p.op("v", lambda e: e.tensor_reduce(
                out=ez16.unsqueeze(2),
                in_=u2.rearrange("p (h i) -> p h i", i=HD),
                axis=AX.X, op=OP.add))
            tcs = p.op("v", lambda e: e.tensor_scalar(
                out=ez16, in0=ez16, scalar1=1.0, scalar2=-1.0,
                op0=OP.min, op1=OP.max))
            p.wait("s", tcs)
            tex = p.op("s", lambda e: e.activation(
                out=ez16, in_=ez16, func=AF.Exp))
            p.wait("v", tex)
            tkez = p.op("v", lambda e: e.tensor_tensor(
                out=kez.rearrange("p (h i) -> p h i", i=HD).transpose([0, 2, 1]),
                in0=ktile.rearrange("p (h i) -> p h i", i=HD).transpose([0, 2, 1]),
                in1=ez16.unsqueeze(1).to_broadcast([128, HD, H]),
                op=OP.mult))
            prevC["k"] = tkez
            p.wait("v", tzm_prev)
            tzb = p.op("v", lambda e: e.tensor_copy(ez_bf, ez16))
            p.wait("t", tzb)
            tzm = p.op("t", lambda e, tt=tt: e.matmul(
                pb[4][0:H, 0:1], ez_bf, sb_ones_col_bf,
                start=(tt == 0), stop=(tt == T // 128 - 1)))
            tzm_prev = tzm
            p.wait("v", tv)
            p.wait("t", tkez)
            p.wait("t", tv)
            tvb = None
            for hp in range(DT):
                bank = pb[5]
                h0, h1 = 2 * hp, 2 * hp + 1
                tkv0 = p.op("t", lambda e, hp=hp, bank=bank: e.matmul(
                    bank[0:64, hp * 64:(hp + 1) * 64],
                    kez[:, (2 * hp) * 64:(2 * hp + 1) * 64],
                    vtile[:, (2 * hp) * 64:(2 * hp + 1) * 64],
                    start=True, stop=True))
                tkv1 = p.op("t", lambda e, hp=hp, bank=bank: e.matmul(
                    bank[64:128, hp * 64:(hp + 1) * 64],
                    kez[:, (2 * hp + 1) * 64:(2 * hp + 2) * 64],
                    vtile[:, (2 * hp + 1) * 64:(2 * hp + 2) * 64],
                    start=True, stop=True))
                p.wait("v", tkv1)
                tvb = p.op("v", lambda e, hp=hp, bank=bank: e.tensor_tensor(
                    out=kv_sb[:, hp * 64:(hp + 1) * 64],
                    in0=kv_sb[:, hp * 64:(hp + 1) * 64],
                    in1=bank[:, hp * 64:(hp + 1) * 64], op=OP.add))
                p.wait("t", tvb)
            prevC["v"] = tvb
        p.wait("v", tzm)
        trz = p.op("v", lambda e: e.reciprocal(out=zrec[:, 0:1], in_=pb[4][0:H, 0:1]))
        tsz = p.op("v", lambda e: e.tensor_scalar(
            out=zrec[:, 0:1], in0=zrec[:, 0:1], scalar1=float(T), scalar2=None,
            op0=OP.mult))
        p.barrier()

        w_wo = ar.alloc([128, DT, D], BF16, "ab_wo")
        r = di["Wo_T"].rearrange("(k p) m -> k p m", p=128)
        two = [p.dma("g", w_wo[:, k], r[k], g="ab_wol") for k in range(DT)]
        kvbf = ar.alloc([128, 512], BF16, "ab_kvbf")
        tkvb = p.op("v", lambda e: e.tensor_copy(kvbf, kv_sb))
        tsi_prev2 = None
        tsa_prev = None
        tmw_last = None
        ofm = ar.alloc([128, DT, TCA], BF16, "ab_ofm")
        resc = ar.alloc([128, DT, TCA], F32, "ab_resc")
        prevD = {}
        for c in range(NCA):
            t0 = c * TCA
            p.wait("q", prevD.get("q"))
            tq = p.dma("q", qfm, ds["r_qb"].rearrange(
                "(k p) t -> p k t", p=128)[:, :, t0:t0 + TCA], g="ab_ql2")
            p.wait("v", tq)
            p.wait("v", prevD.get("ubT"))
            for k in range(DT):
                tu = p.op("v", lambda e, k=k: e.tensor_scalar(
                    out=ub[:, k], in0=qfm[:, k], scalar1=EPSB,
                    scalar2=ksum[:, k:k + 1], op0=OP.add, op1=OP.mult))
            p.wait("t", tu)
            p.wait("t", tsi_prev2)
            for k in range(DT):
                tmsi = p.op("t", lambda e, k=k: e.matmul(
                    pb[4][0:H, 0:TCA], mask1[:, k], ub[:, k],
                    start=(k == 0), stop=(k == DT - 1)))
            p.wait("v", tmsi)
            tsi = p.op("v", lambda e: e.reciprocal(
                out=si_sb, in_=pb[4][0:H, 0:TCA]))
            tsi_prev2 = tsi
            for k in range(DT):
                tu2 = p.op("v", lambda e, k=k: e.tensor_scalar(
                    out=ub[:, k], in0=qfm[:, k], scalar1=EPSB,
                    scalar2=ksosum[:, k:k + 1], op0=OP.add, op1=OP.mult))
            p.wait("t", tu2)
            p.wait("t", tsa_prev)
            for k in range(DT):
                tms = p.op("t", lambda e, k=k: e.matmul(
                    pb[4][0:H, TCA:2 * TCA] if False else pb[5][0:H, 0:TCA],
                    mask1[:, k], ub[:, k],
                    start=(k == 0), stop=(k == DT - 1)))
            p.wait("s", tms)
            tsa = p.op("s", lambda e: e.activation(
                out=sa_sb, in_=pb[5][0:H, 0:TCA], func=AF.Sigmoid))
            tsa_prev = tsa
            p.wait("v", tsa)
            p.op("v", lambda e: e.tensor_tensor(
                out=sa_sb, in0=sa_sb, in1=si_sb, op=OP.mult))
            tss = p.op("v", lambda e: e.tensor_scalar(
                out=sis_bf, in0=sa_sb, scalar1=zrec[:, 0:1], scalar2=None,
                op0=OP.mult))
            p.wait("t", tss)
            for k in range(DT):
                tmsb = p.op("t", lambda e, k=k: e.matmul(
                    pb[6][:, 0:TCA], mask2[:, k], sis_bf, start=True, stop=True))
                p.wait("v", tmsb)
                tqs2 = p.op("v", lambda e, k=k: e.tensor_tensor(
                    out=ub[:, k], in0=qfm[:, k], in1=pb[6][:, 0:TCA], op=OP.mult))
                p.wait("t", tqs2)
            p.wait("t", tkvb)
            p.wait_many("t", two if c == 0 else [])
            for h_ in range(H):
                k_, half = h_ // 2, h_ % 2
                bank = pb[half]
                tmo = p.op("t", lambda e, h_=h_, k_=k_, half=half, bank=bank: e.matmul(
                    bank[half * 64:(half + 1) * 64, 0:TCA],
                    kvbf[half * 64:(half + 1) * 64, k_ * 64:(k_ + 1) * 64],
                    ub[half * 64:(half + 1) * 64, k_, :],
                    start=True, stop=True))
                p.wait("v", tmo)
                p.wait("v", prevD.get("ofm"))
                tof = p.op("v", lambda e, h_=h_, k_=k_, half=half, bank=bank: e.tensor_copy(
                    ofm[half * 64:(half + 1) * 64, k_],
                    bank[half * 64:(half + 1) * 64, 0:TCA]))
                p.wait("t", tof)
            prevD["ofm"] = tof
            del tof
            p.wait("q", prevD.get("res"))
            trl = p.dma("q", resc, ds["r_res"].rearrange(
                "(k p) t -> p k t", p=128)[:, :, t0:t0 + TCA], g="ab_rl")
            for mt in range(DT):
                bank = pb[2 + mt % 2]
                for k in range(DT):
                    tmw = p.op("t", lambda e, mt=mt, k=k, bank=bank: e.matmul(
                        bank, w_wo[:, k, mt * 128:(mt + 1) * 128], ofm[:, k],
                        start=(k == 0), stop=(k == DT - 1)))
                tmw_last = tmw
                p.wait("v", tmw)
                p.wait("v", trl)
                tadd = p.op("v", lambda e, mt=mt, bank=bank: e.scalar_tensor_tensor(
                    out=resc[:, mt], in0=bank, scalar=bo_c[:, mt:mt + 1],
                    in1=resc[:, mt], op0=OP.add, op1=OP.add))
                p.wait("t", tadd)
            p.wait("q", tadd)
            tres = p.dma("q", ds["r_res"].rearrange(
                "(k p) t -> p k t", p=128)[:, :, t0:t0 + TCA], resc, g="ab_rs")
            prevD["res"] = tres
            prevD["q"] = tqs2
            prevD["ubT"] = tmw_last
            prevD["ofm"] = tmw_last
        p.barrier()
        ar.pop()

    if "ATTB" in phases:
        _phase_attb()

    def _phase_out():
        ar.push()
        ident_f32 = ar.alloc([128, 128], F32, "ident_f32")
        tid = p.dma("q", ident_f32, c_ident, g="out_id")
        resc = ar.alloc([128, DT, 128], F32, "out_res")
        orow = ar.alloc([128, D], F32, "out_row")
        NTT = T // 128
        prev_st = None
        prev_ld = None
        p.wait_many("t", [tid])
        for tt in range(NTT):
            p.wait("q", prev_ld)
            tld = p.dma("q", resc, ds["r_res"].rearrange(
                "(k p) t -> p k t", p=128)[:, :, tt * 128:(tt + 1) * 128],
                g="out_ld")
            for k in range(DT):
                p.wait("t", tld)
                ttr = p.op("t", lambda e, k=k: e.transpose(
                    pb[k % 4][:, 0:128], resc[:, k], ident_f32))
                p.wait("v", ttr)
                p.wait("v", prev_st)
                tcp = p.op("v", lambda e, k=k: e.tensor_copy(
                    orow[:, k * 128:(k + 1) * 128], pb[k % 4][:, 0:128]))
                p.wait("t", tcp)
            p.wait("q", tcp)
            prev_st = p.dma("q", out_dram[tt * 128:(tt + 1) * 128, :], orow,
                            g="out_st")
            prev_ld = tcp
        p.barrier()
        ar.pop()

    if "OUT" in phases:
        _phase_out()

    if "OUT" in phases:
        pass

    p.barrier()
    p.build_block()
    auto_used = nc.bytes_allocated(bass.MemorySpace.SBUF)
    assert auto_used <= 24 * 1024, f"bass auto-allocs overran into arena: {auto_used}"
    nc.finalize()
    return nc, di, ds, out_dram


_CACHE = {}


def kernel(**inputs):
    shared = _prep_shared(inputs)
    x = np.asarray(inputs["L0_attn_a_x"])
    in_maps = []
    for core in range(NCORES):
        m = dict(shared)
        m["x_fm"] = np.ascontiguousarray(x[core].T.astype(np.float32, copy=False))
        in_maps.append(m)
    nc, di, ds, out_dram = build_program()
    res = run_bass_kernel_spmd(nc, in_maps, core_ids=list(range(NCORES)))
    out = np.stack([r["out_tm"] for r in res.results], axis=0)
    return out.astype(np.float32)
